# version 11
# baseline (speedup 1.0000x reference)
import contextlib
import numpy as np
import ml_dtypes
import concourse.bass as bass
import concourse.mybir as mybir
from concourse.bass_utils import run_bass_kernel_spmd

F32 = mybir.dt.float32
BF16 = mybir.dt.bfloat16
AF = mybir.ActivationFunctionType
ALU = mybir.AluOpType
AX = mybir.AxisListType
BF = ml_dtypes.bfloat16

ENGINES = ["sync", "tensor", "scalar", "vector", "gpsimd"]

S = 2048
D = 1024
NT = S // 128
NDC = D // 128
DH = 4096
NHC = DH // 128
NW = 15


class DepSer:

    KEEP = 160
    NDMA = 24

    def __init__(self, nc, esems, dsems):
        self.nc = nc
        self.esems = esems
        self.dsems = dsems
        self.duse = [0] * len(dsems)
        self.dnext = 0
        self.count = {e: 0 for e in ENGINES}
        self.ops = {e: [] for e in ENGINES}
        self.writers = {}
        self.readers = {}
        self.pruned = {}
        self.waited = {e: {} for e in ENGINES}

    @staticmethod
    def _box(ap):
        t = ap.tensor
        shape = list(t.shape)
        row = 1
        for s in shape[1:]:
            row *= s
        off = ap.offset
        p_lo = off // row if row else 0
        c_lo = off % row if row else 0
        p_ext = 1
        c_ext = 1
        for step, count in ap.ap:
            if count <= 1:
                continue
            ext = step * (count - 1)
            if row and step >= row:
                p_ext += ext // row
            else:
                c_ext += ext
        return t.name, (p_lo, p_lo + p_ext, c_lo, c_lo + c_ext)

    @staticmethod
    def _ovl(a, b):
        return a[0] < b[1] and b[0] < a[1] and a[2] < b[3] and b[2] < a[3]

    def _deps(self, engine, rboxes, wboxes):
        deps = {}

        def need(sname, c):
            if sname == engine and engine == "tensor":
                return
            if c > deps.get(sname, 0):
                deps[sname] = c

        for name, box in rboxes:
            for sn, c in self.pruned.get(name, {}).items():
                need(sn, c)
            for (sn, c, b) in self.writers.get(name, ()):
                if self._ovl(box, b):
                    need(sn, c)
        for name, box in wboxes:
            for sn, c in self.pruned.get(name, {}).items():
                need(sn, c)
            for (sn, c, b) in self.writers.get(name, ()):
                if self._ovl(box, b):
                    need(sn, c)
            for (sn, c, b) in self.readers.get(name, ()):
                if self._ovl(box, b):
                    need(sn, c)
        return deps

    def _record(self, token, rboxes, wboxes):
        sn, c = token
        for name, box in rboxes:
            lst = self.readers.setdefault(name, [])
            lst.append((sn, c, box))
            if len(lst) > self.KEEP:
                s0, c0, _ = lst.pop(0)
                pm = self.pruned.setdefault(name, {})
                if c0 > pm.get(s0, 0):
                    pm[s0] = c0
        for name, box in wboxes:
            lst = self.writers.setdefault(name, [])
            lst.append((sn, c, box))
            if len(lst) > self.KEEP:
                s0, c0, _ = lst.pop(0)
                pm = self.pruned.setdefault(name, {})
                if c0 > pm.get(s0, 0):
                    pm[s0] = c0

    def _effwaits(self, engine, deps):
        w = self.waited[engine]
        eff = []
        for sn, c in sorted(deps.items()):
            if c > w.get(sn, 0):
                eff.append((sn, c))
                w[sn] = c
        return eff

    def emit(self, engine, fn, inc=1, rd=(), wr=()):
        rboxes = [self._box(ap) for ap in rd]
        wboxes = [self._box(ap) for ap in wr]
        deps = self._deps(engine, rboxes, wboxes)
        if inc == 16:
            k = self.dnext
            self.dnext = (self.dnext + 1) % len(self.dsems)
            prev = self.duse[k] * 16
            sname = "d%d" % k
            if prev > 0 and prev > deps.get(sname, 0):
                deps[sname] = prev
            self.duse[k] += 1
            token = (sname, self.duse[k] * 16)
            eff = self._effwaits(engine, deps)
            self.ops[engine].append((eff, fn, ("d", k)))
        else:
            self.count[engine] += inc
            token = (engine, self.count[engine])
            eff = self._effwaits(engine, deps)
            self.ops[engine].append((eff, fn, ("e", engine)))
        self._record(token, rboxes, wboxes)

    def final_barrier(self):
        deps = {e: c for e, c in self.count.items() if c > 0 and e != "sync"}
        for k, u in enumerate(self.duse):
            if u > 0:
                deps["d%d" % k] = u * 16
        eff = sorted(deps.items())
        self.count["sync"] += 1
        self.ops["sync"].append((eff, lambda e: e.nop(), ("e", "sync")))

    def _sem(self, sname):
        if sname.startswith("d") and sname[1:].isdigit():
            return self.dsems[int(sname[1:])]
        return self.esems[sname]

    def finalize(self):
        nc = self.nc
        with nc.Block() as block:
            def mk(engine_name):
                def body(eng):
                    for eff, fn, tok in self.ops[engine_name]:
                        for sn, c in eff:
                            eng.wait_ge(self._sem(sn), c)
                        inst = fn(eng)
                        if tok[0] == "d":
                            inst.then_inc(self.dsems[tok[1]], 16)
                        else:
                            inst.then_inc(self.esems[tok[1]], 1)
                return body
            block.sync(mk("sync"))
            block.tensor(mk("tensor"))
            block.scalar(mk("scalar"))
            block.vector(mk("vector"))
            block.gpsimd(mk("gpsimd"))


def _aps(*xs):
    return [x for x in xs if x is not None and hasattr(x, "tensor")]


class H:

    def __init__(self, ser):
        self.E = ser.emit

    def DMA(self, out, in_, q="sync"):
        self.E(q, lambda e, o=out, i=in_: e.dma_start(out=o, in_=i), 16,
               rd=_aps(in_), wr=_aps(out))

    def DMAT(self, out, in_, q="scalar"):
        self.E(q, lambda e, o=out, i=in_: e.dma_start_transpose(out=o, in_=i), 16,
               rd=_aps(in_), wr=_aps(out))

    def MM(self, out, l, r, start=True, stop=True):
        rd = _aps(l, r) if start else _aps(l, r, out)
        self.E("tensor", lambda e, o=out, a=l, b=r, s=start, p=stop:
               e.matmul(o, a, b, start=s, stop=p), 1, rd=rd, wr=_aps(out))

    def TR(self, out, in_, ident):
        self.E("tensor", lambda e, o=out, i=in_, d=ident: e.transpose(o, i, d), 1,
               rd=_aps(in_, ident), wr=_aps(out))

    def CP(self, out, in_):
        self.E("vector", lambda e, o=out, i=in_: e.tensor_copy(out=o, in_=i), 1,
               rd=_aps(in_), wr=_aps(out))

    def ACT(self, out, in_, func, bias=None, scale=None, accum=None):
        kw = {}
        if bias is not None:
            kw["bias"] = bias
        if scale is not None:
            kw["scale"] = scale
        if accum is not None:
            kw["accum_out"] = accum
        self.E("scalar", lambda e, o=out, i=in_, f=func, k=kw:
               e.activation(out=o, in_=i, func=f, **k), 1,
               rd=_aps(in_, bias if hasattr(bias, "tensor") else None,
                       scale if hasattr(scale, "tensor") else None),
               wr=_aps(out, accum))

    def SMUL(self, out, in_, c):
        self.E("scalar", lambda e, o=out, i=in_, m=c: e.mul(o, i, m), 1,
               rd=_aps(in_), wr=_aps(out))

    def TT(self, out, in0, in1, op):
        self.E("vector", lambda e, o=out, a=in0, b=in1, p=op:
               e.tensor_tensor(out=o, in0=a, in1=b, op=p), 1,
               rd=_aps(in0, in1), wr=_aps(out))

    def TSMUL(self, out, in0, s1):
        self.E("vector", lambda e, o=out, a=in0, s=s1: e.tensor_scalar_mul(out=o, in0=a, scalar1=s), 1,
               rd=_aps(in0, s1 if hasattr(s1, "tensor") else None), wr=_aps(out))

    def TS(self, out, in0, s1, s2, op0, op1):
        self.E("vector", lambda e, o=out, a=in0, x=s1, y=s2, p0=op0, p1=op1:
               e.tensor_scalar(out=o, in0=a, scalar1=x, scalar2=y, op0=p0, op1=p1), 1,
               rd=_aps(in0, s1 if hasattr(s1, "tensor") else None,
                       s2 if hasattr(s2, "tensor") else None), wr=_aps(out))

    def TSMAX(self, out, in0, s1):
        self.E("vector", lambda e, o=out, a=in0, s=s1: e.tensor_scalar_max(out=o, in0=a, scalar1=s), 1,
               rd=_aps(in0, s1 if hasattr(s1, "tensor") else None), wr=_aps(out))

    def TSSUB(self, out, in0, s1):
        self.E("vector", lambda e, o=out, a=in0, s=s1: e.tensor_scalar_sub(out=o, in0=a, scalar1=s), 1,
               rd=_aps(in0, s1 if hasattr(s1, "tensor") else None), wr=_aps(out))

    def TSADD(self, out, in0, s1):
        self.E("vector", lambda e, o=out, a=in0, s=s1: e.tensor_scalar_add(out=o, in0=a, scalar1=s), 1,
               rd=_aps(in0, s1 if hasattr(s1, "tensor") else None), wr=_aps(out))

    def RSUM(self, out, in_):
        self.E("vector", lambda e, o=out, i=in_: e.reduce_sum(o, i, axis=AX.X), 1,
               rd=_aps(in_), wr=_aps(out))

    def RMAX(self, out, in_, absv=None):
        self.E("vector", lambda e, o=out, i=in_, a=absv:
               e.tensor_reduce(o, i, axis=AX.X, op=ALU.max, apply_absolute_value=a), 1,
               rd=_aps(in_), wr=_aps(out))

    def RADD(self, out, in_, absv=None):
        self.E("vector", lambda e, o=out, i=in_, a=absv:
               e.tensor_reduce(o, i, axis=AX.X, op=ALU.add, apply_absolute_value=a), 1,
               rd=_aps(in_), wr=_aps(out))

    def RECIP(self, out, in_):
        self.E("vector", lambda e, o=out, i=in_: e.reciprocal(o, i), 1,
               rd=_aps(in_), wr=_aps(out))

    def LN(self, t_ap, out_ap, xc, sq, stats, W=1024, eps=1e-5, out_scale=None):
        c = 1.0 if out_scale is None else out_scale
        self.RSUM(stats[:, 0:1], t_ap)
        self.SMUL(stats[:, 1:2], stats[:, 0:1], 1.0 / W)
        self.TSSUB(xc, t_ap, stats[:, 1:2])
        self.ACT(sq, xc, AF.Square, accum=stats[:, 2:3])
        self.TS(stats[:, 3:4], stats[:, 2:3], c * c / W, float(eps), ALU.mult, ALU.add)
        self.ACT(stats[:, 4:5], stats[:, 3:4], AF.Sqrt)
        self.RECIP(stats[:, 5:6], stats[:, 4:5])
        if out_scale is not None:
            self.SMUL(stats[:, 6:7], stats[:, 5:6], c)
            self.TSMUL(out_ap, xc, stats[:, 6:7])
        else:
            self.TSMUL(out_ap, xc, stats[:, 5:6])


def build_megakernel(dbg=False):
    nc = bass.Bass("TRN2", target_bir_lowering=False)
    x = nc.dram_tensor("x", [S, D], F32, kind="ExternalInput").ap()
    Wq = nc.dram_tensor("Wq", [D, D], F32, kind="ExternalInput").ap()
    Wk = nc.dram_tensor("Wk", [D, D], F32, kind="ExternalInput").ap()
    Wv = nc.dram_tensor("Wv", [D, D], F32, kind="ExternalInput").ap()
    Wo = nc.dram_tensor("Wo", [D, D], F32, kind="ExternalInput").ap()
    w1w = nc.dram_tensor("w1w", [DH, D], F32, kind="ExternalInput").ap()
    w1s = nc.dram_tensor("w1s", [DH], F32, kind="ExternalInput").ap()
    w2w = nc.dram_tensor("w2w", [DH, D], F32, kind="ExternalInput").ap()
    w2s = nc.dram_tensor("w2s", [DH], F32, kind="ExternalInput").ap()
    h1w = nc.dram_tensor("h1w", [DH, DH], F32, kind="ExternalInput").ap()
    h1s = nc.dram_tensor("h1s", [DH], F32, kind="ExternalInput").ap()
    h2w = nc.dram_tensor("h2w", [DH, DH], F32, kind="ExternalInput").ap()
    h2s = nc.dram_tensor("h2s", [DH], F32, kind="ExternalInput").ap()
    w3w = nc.dram_tensor("w3w", [D, DH], F32, kind="ExternalInput").ap()
    w3s = nc.dram_tensor("w3s", [D], F32, kind="ExternalInput").ap()
    Qh = nc.dram_tensor("Qh", [64, D, 16], F32, kind="ExternalInput").ap()
    Kh = nc.dram_tensor("Kh", [64, D, 16], F32, kind="ExternalInput").ap()
    Vh = nc.dram_tensor("Vh", [64, D, 16], F32, kind="ExternalInput").ap()
    Wh = nc.dram_tensor("Wh", [D, 64, 16], F32, kind="ExternalInput").ap()
    wgt = nc.dram_tensor("wgt", [D, D], F32, kind="ExternalInput").ap()
    lng = nc.dram_tensor("lng", [D], F32, kind="ExternalInput").ap()
    lnb = nc.dram_tensor("lnb", [D], F32, kind="ExternalInput").ap()
    identb = nc.dram_tensor("identb", [128, 128], BF16, kind="ExternalInput").ap()
    identf = nc.dram_tensor("identf", [128, 128], F32, kind="ExternalInput").ap()
    onesb = nc.dram_tensor("onesb", [128, 1], BF16, kind="ExternalInput").ap()
    onesf = nc.dram_tensor("onesf", [1, 128], F32, kind="ExternalInput").ap()
    cmask = nc.dram_tensor("cmask", [128, 896], BF16, kind="ExternalInput").ap()
    out = nc.dram_tensor("out", [S, D], F32, kind="ExternalOutput").ap()

    _k = "ExternalOutput" if dbg else "Internal"
    x1d = nc.dram_tensor("x1d", [S, D], F32, kind=_k).ap()
    x2d = nc.dram_tensor("x2d", [S, D], F32, kind=_k).ap()
    x3d = nc.dram_tensor("x3d", [S, D], F32, kind=_k).ap()
    zd = nc.dram_tensor("zd", [S, D], F32, kind=_k).ap()
    bTd = nc.dram_tensor("bTd", [NHC, 128, S], BF16, kind=_k).ap()
    cTd = nc.dram_tensor("cTd", [NHC, 128, S], BF16, kind=_k).ap()
    aTd = nc.dram_tensor("aTd", [NHC, 128, S], BF16, kind=_k).ap() if dbg else None

    SL2 = 18432
    with contextlib.ExitStack() as st:
        ab = st.enter_context(nc.sbuf_tensor("ab", [128, 65536 + SL2], BF16))
        af = st.enter_context(nc.sbuf_tensor("af", [128, 9216], F32))
        idb = st.enter_context(nc.sbuf_tensor("idb", [128, 128], BF16))
        idf = st.enter_context(nc.sbuf_tensor("idf", [128, 128], F32))
        onb = st.enter_context(nc.sbuf_tensor("onb", [128, 1], BF16))
        onf = st.enter_context(nc.sbuf_tensor("onf", [1, 128], F32))
        cmk = st.enter_context(nc.sbuf_tensor("cmk", [128, 896], BF16))
        stats = st.enter_context(nc.sbuf_tensor("stats", [128, 8], F32))
        xnb = st.enter_context(nc.sbuf_tensor("xnb", [128, D], BF16))
        p0 = st.enter_context(nc.sbuf_tensor("p0", [128, 512], BF16))
        s1a = st.enter_context(nc.sbuf_tensor("s1a", [128, 32], F32))
        s2a = st.enter_context(nc.sbuf_tensor("s2a", [128, 32], F32))
        mcols = st.enter_context(nc.sbuf_tensor("mcols", [128, 16], F32))
        mpart = st.enter_context(nc.sbuf_tensor("mpart", [128, 1], F32))
        m1 = st.enter_context(nc.sbuf_tensor("m1", [1, 1], F32))
        sc = st.enter_context(nc.sbuf_tensor("sc", [1, 16], F32))
        apart = st.enter_context(nc.sbuf_tensor("apart", [128, 1], F32))
        rbp = st.enter_context(nc.sbuf_tensor("rbp", [128, 1], F32))
        c1p = st.enter_context(nc.sbuf_tensor("c1p", [128, 1], F32))
        fsp = st.enter_context(nc.sbuf_tensor("fsp", [128, 1], F32))
        ps_trb = st.enter_context(nc.psum_tensor("ps_trb", [128, 128], BF16))
        ps_trb2 = st.enter_context(nc.psum_tensor("ps_trb2", [128, 128], BF16))
        p1 = st.enter_context(nc.psum_tensor("p1", [128, 512], F32))
        p2 = st.enter_context(nc.psum_tensor("p2", [128, 512], F32))
        p3 = st.enter_context(nc.psum_tensor("p3", [128, 512], F32))
        p4 = st.enter_context(nc.psum_tensor("p4", [1, 512], F32))
        p5 = st.enter_context(nc.psum_tensor("p5", [64, 512], F32))
        p7 = st.enter_context(nc.psum_tensor("p7", [128, 512], F32))
        esems = {e: st.enter_context(nc.semaphore("sem_" + e)) for e in ENGINES}
        dsems = [st.enter_context(nc.semaphore("dma_%d" % i)) for i in range(DepSer.NDMA)]

        ser = DepSer(nc, esems, dsems)
        h = H(ser)

        def SB(a, b):
            return ab[:, a:b]
        S0, S1B, S2B = 0, 32768, 65536

        def FS(i, n=1):
            return af[:, i * 1024:(i + n) * 1024]

        pTa = p0[:, 0:256]
        pTb = p0[:, 256:512]
        cfv = af[0:64, 4096:4608]
        cf2 = cfv[:, 0:256]
        rden = af[0:1, 4608:5120]
        rden2 = rden[:, 0:256]
        gbv = af[0:1, 5120:6144]
        qsflat = af[0:64, 5120:7168]
        qview = af[0:64, 5120:7168].rearrange("p (d h) -> p d h", h=16)
        wsflat = af[:, 5120:6144]
        wview = af[:, 5120:6144].rearrange("p (v h) -> p v h", h=16)
        xqb = ab[:, 65536 + 8192: 65536 + 9216]
        xqT = ab[:, 65536 + 9216: 65536 + 10240]
        exb = ab[0:64, 81920:82048]
        wrb = ab[:, 82048:82176]
        svs = af[0:32, 5120:5248]
        mrow = af[0:1, 6144:6272]
        arow = af[0:1, 3072:3200]
        acol = af[:, 3200:3208]

        h.DMA(idb[:], identb[:, :])
        h.DMA(idf[:], identf[:, :])
        h.DMA(onb[:], onesb[:, :])
        h.DMA(onf[:], onesf[:, :])
        h.DMA(cmk[:], cmask[:, :])

        xnT = SB(S0, S0 + 16384)
        qT = SB(S0 + 16384, S0 + 32768)
        kT = SB(S1B, S1B + 16384)
        vN = SB(S1B + 16384, S1B + 32768)
        wbf = SB(S2B, S2B + 8192)
        ctxT = SB(S2B + 8192, S2B + 10240)
        xt, xc, sqs, comb = FS(0), FS(1), FS(2), FS(3)
        wstg1 = FS(4)
        wob = FS(5, 4)

        def ln_tileT(src_dram, ti, dstT):
            h.DMA(xt[:], src_dram[ti * 128:(ti + 1) * 128, :])
            h.LN(xt[:], xnb[:], xc[:], sqs[:], stats)
            for dc in range(NDC):
                tb = (ps_trb if dc % 2 == 0 else ps_trb2)[:]
                h.TR(tb, xnb[:, dc * 128:(dc + 1) * 128], idb[:])
                h.CP(dstT[:, dc * S + ti * 128: dc * S + (ti + 1) * 128], tb)

        for ti in range(NT):
            ln_tileT(x, ti, xnT)

        def load_w_nat(w_ap, dst):
            for t in range(NDC):
                h.DMA(wstg1[:], w_ap[t * 128:(t + 1) * 128, :])
                h.CP(dst[:, t * D:(t + 1) * D], wstg1[:])

        def proj_T(dst):
            for et in range(NDC):
                for sch in range(4):
                    P = p1 if (et * 4 + sch) % 2 == 0 else p7
                    for dc in range(NDC):
                        h.MM(P[:], wbf[:, dc * D + et * 128: dc * D + (et + 1) * 128],
                             xnT[:, dc * S + sch * 512: dc * S + (sch + 1) * 512],
                             start=(dc == 0), stop=(dc == NDC - 1))
                    h.ACT(dst[:, et * S + sch * 512: et * S + (sch + 1) * 512], P[:], AF.Copy)

        load_w_nat(Wq, wbf)
        proj_T(qT)
        load_w_nat(Wk, wbf)
        proj_T(kT)
        load_w_nat(Wv, wbf)
        for ti in range(NT):
            for ec in range(2):
                P = p1 if (ti * 2 + ec) % 2 == 0 else p7
                for dc in range(NDC):
                    h.MM(P[:], xnT[:, dc * S + ti * 128: dc * S + (ti + 1) * 128],
                         wbf[:, dc * D + ec * 512: dc * D + (ec + 1) * 512],
                         start=(dc == 0), stop=(dc == NDC - 1))
                h.ACT(vN[:, ti * D + ec * 512: ti * D + (ec + 1) * 512], P[:], AF.Copy)
        load_w_nat(Wo, wbf)

        def emit_chunk_A(cidx, half_ap, other_ap, scale_c):
            if other_ap is None:
                h.LN(half_ap, comb[:], xc[:], sqs[:], stats, out_scale=scale_c)
            else:
                h.TT(comb[:], half_ap, other_ap, ALU.add)
                h.LN(comb[:], comb[:], xc[:], sqs[:], stats, out_scale=scale_c)
            h.DMA(xt[:], x[cidx * 128:(cidx + 1) * 128, :])
            h.TT(xc[:], comb[:], xt[:], ALU.add)
            h.DMA(x1d[cidx * 128:(cidx + 1) * 128, :], xc[:], q="gpsimd")

        for n in range(NW):
            base = (n % 2) * 2 * D
            pbase = ((n + 1) % 2) * 2 * D
            s0q = n * 128
            for hh in range(16):
                hc, hb = hh // 2, (hh % 2) * 64
                for kt_i, pbuf, PS in ((0, pTa, p2), (1, pTb, p7)):
                    kk = s0q + kt_i * 128
                    h.MM(PS[:, 0:256], kT[hb:hb + 64, hc * S + kk: hc * S + kk + 128],
                         qT[hb:hb + 64, hc * S + s0q: hc * S + s0q + 256])
                    h.ACT(pbuf, PS[:, 0:256], AF.Exp, scale=0.125)
                h.MM(p4[:, 0:256], onb[:, :], pTa, start=True, stop=False)
                h.MM(p4[:, 0:256], onb[:, :], pTb, start=False, stop=True)
                h.RECIP(rden2, p4[:, 0:256])
                h.MM(p5[:, 0:256], onf[0:1, 0:64], rden2)
                for kt_i, pbuf in ((0, pTa), (1, pTb)):
                    h.MM(p3[hb:hb + 64, 0:256],
                         vN[:, (n + kt_i) * D + hh * 64: (n + kt_i) * D + hh * 64 + 64],
                         pbuf, start=(kt_i == 0), stop=(kt_i == 1))
                h.CP(cf2, p3[hb:hb + 64, 0:256])
                h.TT(ctxT[hb:hb + 64, hc * 256:(hc + 1) * 256], cf2, p5[:, 0:256], ALU.mult)
            for qh in range(2):
                for ec in range(2):
                    for dc in range(NDC):
                        h.MM(p1[:], ctxT[:, dc * 256 + qh * 128: dc * 256 + (qh + 1) * 128],
                             wbf[:, dc * D + ec * 512: dc * D + (ec + 1) * 512],
                             start=(dc == 0), stop=(dc == NDC - 1))
                    h.CP(wob[:, base + qh * D + ec * 512: base + qh * D + (ec + 1) * 512], p1[:])
            if n == 0:
                emit_chunk_A(0, wob[:, base: base + D], None, 1.0)
            else:
                emit_chunk_A(n, wob[:, base: base + D], wob[:, pbase + D: pbase + 2 * D], 0.5)
        lastb = ((NW - 1) % 2) * 2 * D
        emit_chunk_A(NW, wob[:, lastb + D: lastb + 2 * D], None, 1.0)

        aT = SB(S0, S0 + 65536)
        xnTB = SB(S2B + 2048, S2B + SL2)
        w1T = SB(S2B, S2B + 1024)
        w2T = SB(S2B + 1024, S2B + 2048)
        tA, tB2 = FS(3)[:, 0:512], FS(3)[:, 512:1024]

        for sv, dst in ((w1s, s1a), (w2s, s2a)):
            h.DMA(svs, sv.rearrange("(t f) -> t f", t=32))
            h.TR(p2[:, 0:32], svs, idf[0:32, 0:32])
            h.CP(dst[:], p2[:, 0:32])
        for ti in range(NT):
            ln_tileT(x1d, ti, xnTB)
        for t in range(NHC):
            h.DMA(wstg1[:], w1w[t * 128:(t + 1) * 128, :])
            h.CP(xnb[:], wstg1[:])
            for dc in range(NDC):
                h.DMAT(w1T[:, dc * 128:(dc + 1) * 128], xnb[:, dc * 128:(dc + 1) * 128],
                       q="scalar")
            h.DMA(wstg1[:], w2w[t * 128:(t + 1) * 128, :])
            h.CP(xnb[:], wstg1[:])
            for dc in range(NDC):
                h.DMAT(w2T[:, dc * 128:(dc + 1) * 128], xnb[:, dc * 128:(dc + 1) * 128],
                       q="scalar")
            for sch in range(4):
                P1 = p1 if sch % 2 == 0 else p7
                P2 = p2 if sch % 2 == 0 else p3
                for dc in range(NDC):
                    h.MM(P1[:], w1T[:, dc * 128:(dc + 1) * 128],
                         xnTB[:, dc * S + sch * 512: dc * S + (sch + 1) * 512],
                         start=(dc == 0), stop=(dc == NDC - 1))
                for dc in range(NDC):
                    h.MM(P2[:], w2T[:, dc * 128:(dc + 1) * 128],
                         xnTB[:, dc * S + sch * 512: dc * S + (sch + 1) * 512],
                         start=(dc == 0), stop=(dc == NDC - 1))
                h.ACT(tA[:], P1[:], AF.Silu, scale=s1a[:, t:t + 1])
                h.TSMUL(tB2[:], P2[:], s2a[:, t:t + 1])
                h.TT(aT[:, t * S + sch * 512: t * S + (sch + 1) * 512], tA[:], tB2[:], ALU.mult)

        hinS = SB(S0, S0 + 65536)
        wTh = SB(S2B, S2B + 4096)
        wbfh = SB(S2B + 4096, S2B + 8192)
        wstg4 = FS(5, 4)
        obf = SB(S2B + 8192, S2B + 8704)
        obf2 = SB(S2B + 8704, S2B + 9216)

        def hidden_layer(houtd, hwt, hst):
            h.DMA(svs, hst.rearrange("(t f) -> t f", t=32))
            h.TR(p2[:, 0:32], svs, idf[0:32, 0:32])
            h.CP(s1a[:], p2[:, 0:32])
            for t in range(NHC):
                for q4 in range(4):
                    h.DMA(wstg4[:, q4 * D:(q4 + 1) * D], hwt[t * 128:(t + 1) * 128, q4 * D:(q4 + 1) * D])
                h.CP(wbfh[:], wstg4[:])
                for ic in range(NHC):
                    h.DMAT(wTh[:, ic * 128:(ic + 1) * 128], wbfh[:, ic * 128:(ic + 1) * 128],
                           q="scalar")
                for sch in range(4):
                    P = p1 if sch % 2 == 0 else p7
                    ob = obf if sch % 2 == 0 else obf2
                    for ic in range(NHC):
                        h.MM(P[:], wTh[:, ic * 128:(ic + 1) * 128],
                             hinS[:, ic * S + sch * 512: ic * S + (sch + 1) * 512],
                             start=(ic == 0), stop=(ic == NHC - 1))
                    h.ACT(ob, P[:], AF.Gelu, scale=s1a[:, t:t + 1])
                    h.DMA(houtd[t, :, sch * 512:(sch + 1) * 512], ob, q="gpsimd")

        if dbg:
            for ic in range(NHC):
                h.DMA(aTd[ic, :, :], aT[:, ic * S:(ic + 1) * S], q="gpsimd")
        hidden_layer(bTd, h1w, h1s)
        for ic in range(NHC):
            h.DMA(hinS[:, ic * S:(ic + 1) * S], bTd[ic, :, :])
        hidden_layer(cTd, h2w, h2s)

        w3R = SB(S1B, S1B + 32768)
        cS = SB(S2B + 8192, S2B + 12288)
        cS2 = SB(S2B + 12288, S2B + 16384)
        ys = FS(3)
        s3B = FS(4)

        h.DMA(gbv, w3s[None, :])
        for ec in range(2):
            h.MM(p1[:], onf[0:1, :], gbv[:, ec * 512:(ec + 1) * 512])
            h.CP(s3B[:, ec * 512:(ec + 1) * 512], p1[:])
        for et in range(NDC):
            for q4 in range(4):
                h.DMA(wstg4[:, q4 * D:(q4 + 1) * D], w3w[et * 128:(et + 1) * 128, q4 * D:(q4 + 1) * D])
            h.CP(wbfh[:], wstg4[:])
            for ic in range(NHC):
                h.DMAT(w3R[:, ic * D + et * 128: ic * D + (et + 1) * 128],
                       wbfh[:, ic * 128:(ic + 1) * 128],
                       q="scalar")
        for ti in range(NT):
            cSt = cS if ti % 2 == 0 else cS2
            for ic in range(NHC):
                h.DMA(cSt[:, ic * 128:(ic + 1) * 128], cTd[ic, :, ti * 128:(ti + 1) * 128])
            for ec in range(2):
                P = p1 if ec == 0 else p7
                for ic in range(NHC):
                    h.MM(P[:], cSt[:, ic * 128:(ic + 1) * 128],
                         w3R[:, ic * D + ec * 512: ic * D + (ec + 1) * 512],
                         start=(ic == 0), stop=(ic == NHC - 1))
                h.TT(ys[:, ec * 512:(ec + 1) * 512], s3B[:, ec * 512:(ec + 1) * 512], P[:], ALU.mult)
            h.LN(ys[:], comb[:], xc[:], sqs[:], stats)
            h.DMA(xt[:], x1d[ti * 128:(ti + 1) * 128, :])
            h.TT(xc[:], comb[:], xt[:], ALU.add)
            h.DMA(x2d[ti * 128:(ti + 1) * 128, :], xc[:], q="gpsimd")

        bufA = SB(S0, S0 + 16384)
        qTs = SB(S0 + 16384, S0 + 32768)
        kTs = SB(S1B, S1B + 16384)
        vS = SB(S1B + 16384, S1B + 32768)
        bufE = SB(S2B, S2B + 8192)
        bufF = SB(S2B + 8192, S2B + 16384)
        ztF = FS(3)
        SQ = 0.25

        for ti in range(NT):
            ln_tileT(x2d, ti, bufA)

        def prep_qk(src, dstT):
            for dc in range(8):
                h.DMA(qsflat, src[:, dc * 128:(dc + 1) * 128, :].rearrange("q d h -> q (d h)"))
                for hh in range(16):
                    tb = (ps_trb if hh % 2 == 0 else ps_trb2)[:, 0:64]
                    h.CP(exb, qview[:, :, hh])
                    h.TR(tb, exb, idb[0:64, 0:64])
                    h.CP(bufF[:, (hh * 8 + dc) * 64:(hh * 8 + dc + 1) * 64], tb)
            for hh in range(16):
                hc, hb = hh // 2, (hh % 2) * 64
                for j in range(4):
                    P = p1 if j % 2 == 0 else p7
                    for dc in range(NDC):
                        h.MM(P[0:64, :], bufF[:, (hh * 8 + dc) * 64:(hh * 8 + dc + 1) * 64],
                             bufA[:, dc * S + j * 512: dc * S + (j + 1) * 512],
                             start=(dc == 0), stop=(dc == NDC - 1))
                    h.ACT(dstT[hb:hb + 64, hc * S + j * 512: hc * S + (j + 1) * 512],
                          P[0:64, :], AF.Copy, scale=SQ)

        prep_qk(Qh, qTs)
        prep_qk(Kh, kTs)

        for dc in range(8):
            h.DMA(qsflat, Vh[:, dc * 128:(dc + 1) * 128, :].rearrange("q d h -> q (d h)"))
            for hh in range(16):
                tb = (ps_trb if hh % 2 == 0 else ps_trb2)[:, 0:64]
                h.CP(exb, qview[:, :, hh])
                h.TR(tb, exb, idb[0:64, 0:64])
                h.CP(bufE[:, dc * D + hh * 64: dc * D + (hh + 1) * 64], tb)
        for ti in range(NT):
            for ec in range(2):
                P = p1 if (ti * 2 + ec) % 2 == 0 else p7
                for dc in range(NDC):
                    h.MM(P[:], bufA[:, dc * S + ti * 128: dc * S + (ti + 1) * 128],
                         bufE[:, dc * D + ec * 512: dc * D + (ec + 1) * 512],
                         start=(dc == 0), stop=(dc == NDC - 1))
                h.ACT(vS[:, ti * D + ec * 512: ti * D + (ec + 1) * 512], P[:], AF.Copy, scale=SQ)
        for dc in range(NDC):
            h.DMA(wsflat, Wh[dc * 128:(dc + 1) * 128, :, :].rearrange("d v h -> d (v h)"))
            for b in range(8):
                tb = (ps_trb if b % 2 == 0 else ps_trb2)[:]
                h.CP(wrb[:, 0:64], wview[:, :, 2 * b])
                h.CP(wrb[:, 64:128], wview[:, :, 2 * b + 1])
                h.TR(tb, wrb, idb[:])
                h.CP(bufE[:, b * D + dc * 128: b * D + (dc + 1) * 128], tb)

        for hh in range(16):
            hc, hb = hh // 2, (hh % 2) * 64
            for j in range(4):
                nC = 4 * (j + 1)
                for c in range(nC):
                    PS = p2 if c % 2 == 0 else p7
                    h.MM(PS[:], kTs[hb:hb + 64, hc * S + c * 128: hc * S + (c + 1) * 128],
                         qTs[hb:hb + 64, hc * S + j * 512: hc * S + (j + 1) * 512])
                    o = c * 128 - j * 512
                    if o >= 0:
                        h.ACT(p0[:], PS[:], AF.Exp, scale=1.0 / 64.0)
                        h.TT(bufF[:, c * 512:(c + 1) * 512], p0[:], cmk[:, 384 - o: 896 - o], ALU.mult)
                    else:
                        h.ACT(bufF[:, c * 512:(c + 1) * 512], PS[:], AF.Exp, scale=1.0 / 64.0)
                for c in range(nC):
                    h.MM(p4[:], onb[:, :], bufF[:, c * 512:(c + 1) * 512],
                         start=(c == 0), stop=(c == nC - 1))
                h.RECIP(rden, p4[:])
                h.MM(p5[:], onf[0:1, 0:64], rden)
                cb = (hh % 2) * 64
                for c in range(nC):
                    h.MM(p3[cb:cb + 64, :], vS[:, c * D + hh * 64: c * D + (hh + 1) * 64],
                         bufF[:, c * 512:(c + 1) * 512], start=(c == 0), stop=(c == nC - 1))
                h.CP(cfv, p3[cb:cb + 64, :])
                h.TT(bufA[hb:hb + 64, hc * S + j * 512: hc * S + (j + 1) * 512], cfv, p5[:], ALU.mult)

        for ti in range(NT):
            for ec in range(2):
                P = p1 if ec == 0 else p7
                for dc in range(NDC):
                    h.MM(P[:], bufA[:, dc * S + ti * 128: dc * S + (ti + 1) * 128],
                         bufE[:, dc * D + ec * 512: dc * D + (ec + 1) * 512],
                         start=(dc == 0), stop=(dc == NDC - 1))
                h.ACT(ztF[:, ec * 512:(ec + 1) * 512], P[:], AF.Copy, scale=0.25)
            h.LN(ztF[:], comb[:], xc[:], sqs[:], stats)
            h.DMA(xt[:], x2d[ti * 128:(ti + 1) * 128, :])
            h.TT(xc[:], comb[:], xt[:], ALU.add)
            h.DMA(x3d[ti * 128:(ti + 1) * 128, :], xc[:], q="gpsimd")

        xn1, xn2, ztG, zt2 = FS(3), FS(4), FS(5), FS(6)
        Gt, Bt = FS(7), FS(8)
        h.DMA(gbv, lng[None, :])
        for ec in range(2):
            h.MM(p1[:], onf[0:1, :], gbv[:, ec * 512:(ec + 1) * 512])
            h.CP(Gt[:, ec * 512:(ec + 1) * 512], p1[:])
        h.DMA(gbv, lnb[None, :])
        for ec in range(2):
            h.MM(p1[:], onf[0:1, :], gbv[:, ec * 512:(ec + 1) * 512])
            h.CP(Bt[:, ec * 512:(ec + 1) * 512], p1[:])
        for ti in range(NT):
            h.DMA(xt[:], x3d[ti * 128:(ti + 1) * 128, :])
            h.LN(xt[:], xn1[:], xc[:], sqs[:], stats)
            h.LN(xn1[:], xn2[:], xc[:], sqs[:], stats, eps=1e-3)
            h.TT(ztG[:], xn2[:], Gt[:], ALU.mult)
            h.TT(zt2[:], ztG[:], Bt[:], ALU.add)
            h.DMA(zd[ti * 128:(ti + 1) * 128, :], zt2[:], q="gpsimd")
            h.RMAX(mcols[:, ti:ti + 1], zt2[:], absv=True)
        h.RMAX(mpart[:], mcols[:])
        h.TR(p2[0:1, 0:128], mpart[:], idf[:])
        h.CP(mrow, p2[0:1, 0:128])
        h.RMAX(m1[:], mrow)
        h.CP(sc[:, 0:1], m1[:])

        tqf, sgn, u2, ztH, xqf, ytH = FS(3), FS(4), FS(5), FS(6), FS(7), FS(8)
        wqb = SB(S2B, S2B + 8192)
        QB = 128.0
        HIc = float(np.float32(QB) - np.float32(1e-5))

        for t in range(NDC):
            h.DMA(wstg1[:], wgt[t * 128:(t + 1) * 128, :])
            h.RADD(acol[:, t:t + 1], wstg1[:], absv=True)
        h.RSUM(apart[:], acol)
        h.TR(p2[0:1, 0:128], apart[:], idf[:])
        h.CP(arow, p2[0:1, 0:128])
        h.RSUM(sc[:, 1:2], arow)
        h.SMUL(sc[:, 2:3], sc[:, 1:2], 1.0 / (1024.0 * 1024.0))
        h.TSADD(sc[:, 3:4], sc[:, 2:3], 1e-5)
        h.RECIP(sc[:, 4:5], sc[:, 3:4])
        h.RECIP(sc[:, 5:6], sc[:, 0:1])
        h.SMUL(sc[:, 6:7], sc[:, 5:6], QB)
        h.TT(sc[:, 7:8], sc[:, 0:1], sc[:, 2:3], ALU.mult)
        h.SMUL(sc[:, 7:8], sc[:, 7:8], 1.0 / QB)
        for src_col, dst in ((4, rbp), (6, c1p), (7, fsp)):
            h.MM(p2[:, 0:1], onf[0:1, :], sc[:, src_col:src_col + 1])
            h.CP(dst[:], p2[:, 0:1])

        for t in range(NDC):
            h.DMA(wstg1[:], wgt[t * 128:(t + 1) * 128, :])
            h.TSMUL(tqf[:], wstg1[:], rbp[:])
            h.ACT(sgn[:], tqf[:], AF.Sign)
            h.ACT(u2[:], tqf[:], AF.Abs)
            h.TSSUB(tqf[:], u2[:], 0.5)
            h.ACT(u2[:], tqf[:], AF.Sign)
            h.ACT(tqf[:], u2[:], AF.Relu)
            h.TT(wqb[:, t * D:(t + 1) * D], sgn[:], tqf[:], ALU.mult)

        for ti in range(NT):
            h.DMA(ztH[:], zd[ti * 128:(ti + 1) * 128, :])
            h.TS(xqf[:], ztH[:], c1p[:], HIc, ALU.mult, ALU.min)
            h.TSMAX(xqb, xqf[:], -HIc)
            for dc in range(NDC):
                tb = (ps_trb if dc % 2 == 0 else ps_trb2)[:]
                h.TR(tb, xqb[:, dc * 128:(dc + 1) * 128], idb[:])
                h.CP(xqT[:, dc * 128:(dc + 1) * 128], tb)
            for ec in range(2):
                P = p1 if ec == 0 else p7
                for dc in range(NDC):
                    h.MM(P[:], xqT[:, dc * 128:(dc + 1) * 128],
                         wqb[:, dc * D + ec * 512: dc * D + (ec + 1) * 512],
                         start=(dc == 0), stop=(dc == NDC - 1))
                h.TSMUL(ytH[:, ec * 512:(ec + 1) * 512], P[:], fsp[:])
            h.LN(ytH[:], comb[:], xc[:], sqs[:], stats)
            h.DMA(xt[:], x3d[ti * 128:(ti + 1) * 128, :])
            h.TT(xc[:], comb[:], xt[:], ALU.add)
            h.DMA(out[ti * 128:(ti + 1) * 128, :], xc[:], q="gpsimd")
        ser.final_barrier()
        ser.finalize()
    return nc


def _consts():
    return {
        "identb": np.eye(128, dtype=BF),
        "identf": np.eye(128, dtype=np.float32),
        "onesb": np.ones((128, 1), dtype=BF),
        "onesf": np.ones((1, 128), dtype=np.float32),
    }


def _cmask():
    p = np.arange(128)[:, None]
    y = np.arange(896)[None, :]
    return (y >= p + 384).astype(BF)


def kernel(L0_attn_hidden_states, L0_attn_Wq, L0_attn_Wk, L0_attn_Wv, L0_attn_Wo,
           L0_ffn_w1_w, L0_ffn_w1_s, L0_ffn_w2_w, L0_ffn_w2_s,
           L0_ffn_h1_w, L0_ffn_h1_s, L0_ffn_h2_w, L0_ffn_h2_s,
           L0_ffn_w3_w, L0_ffn_w3_s,
           L1_attn_Q, L1_attn_K, L1_attn_V, L1_attn_W,
           L1_ffn_weight, L1_ffn_ln_gamma, L1_ffn_ln_beta):
    NCORES = 8
    cores = list(range(NCORES))
    cst = _consts()
    x_all = np.ascontiguousarray(np.asarray(L0_attn_hidden_states, dtype=np.float32))

    shared = {
        "Wq": np.ascontiguousarray(L0_attn_Wq), "Wk": np.ascontiguousarray(L0_attn_Wk),
        "Wv": np.ascontiguousarray(L0_attn_Wv), "Wo": np.ascontiguousarray(L0_attn_Wo),
        "w1w": np.ascontiguousarray(L0_ffn_w1_w), "w1s": np.ascontiguousarray(L0_ffn_w1_s),
        "w2w": np.ascontiguousarray(L0_ffn_w2_w), "w2s": np.ascontiguousarray(L0_ffn_w2_s),
        "h1w": np.ascontiguousarray(L0_ffn_h1_w), "h1s": np.ascontiguousarray(L0_ffn_h1_s),
        "h2w": np.ascontiguousarray(L0_ffn_h2_w), "h2s": np.ascontiguousarray(L0_ffn_h2_s),
        "w3w": np.ascontiguousarray(L0_ffn_w3_w), "w3s": np.ascontiguousarray(L0_ffn_w3_s),
        "Qh": np.ascontiguousarray(L1_attn_Q), "Kh": np.ascontiguousarray(L1_attn_K),
        "Vh": np.ascontiguousarray(L1_attn_V), "Wh": np.ascontiguousarray(L1_attn_W),
        "wgt": np.ascontiguousarray(L1_ffn_weight),
        "lng": np.ascontiguousarray(L1_ffn_ln_gamma), "lnb": np.ascontiguousarray(L1_ffn_ln_beta),
        "identb": cst["identb"], "identf": cst["identf"],
        "onesb": cst["onesb"], "onesf": cst["onesf"], "cmask": _cmask(),
    }
    nc = build_megakernel()
    im = [dict(shared, x=np.ascontiguousarray(x_all[c])) for c in cores]
    res = run_bass_kernel_spmd(nc, im, core_ids=cores)
    return np.stack([res.results[c]["out"] for c in cores], axis=0).astype(np.float32)


# version 12 (final)
# speedup vs baseline: 1.2351x; 1.2351x over previous
import contextlib
import numpy as np
import ml_dtypes
import concourse.bass as bass
import concourse.mybir as mybir
from concourse.bass_utils import run_bass_kernel_spmd

F32 = mybir.dt.float32
BF16 = mybir.dt.bfloat16
AF = mybir.ActivationFunctionType
ALU = mybir.AluOpType
AX = mybir.AxisListType
BF = ml_dtypes.bfloat16

ENGINES = ["sync", "tensor", "scalar", "vector", "gpsimd"]

S = 2048
D = 1024
NT = S // 128
NDC = D // 128
DH = 4096
NHC = DH // 128
NW = 15


class DepSer:

    KEEP = 160
    NDMA = 24

    def __init__(self, nc, esems, dsems):
        self.nc = nc
        self.esems = esems
        self.dsems = dsems
        self.duse = [0] * len(dsems)
        self.dnext = 0
        self.count = {e: 0 for e in ENGINES}
        self.ops = {e: [] for e in ENGINES}
        self.writers = {}
        self.readers = {}
        self.pruned = {}
        self.waited = {e: {} for e in ENGINES}

    @staticmethod
    def _box(ap):
        t = ap.tensor
        shape = list(t.shape)
        row = 1
        for s in shape[1:]:
            row *= s
        off = ap.offset
        p_lo = off // row if row else 0
        c_lo = off % row if row else 0
        p_ext = 1
        c_ext = 1
        for step, count in ap.ap:
            if count <= 1:
                continue
            ext = step * (count - 1)
            if row and step >= row:
                p_ext += ext // row
            else:
                c_ext += ext
        return t.name, (p_lo, p_lo + p_ext, c_lo, c_lo + c_ext)

    @staticmethod
    def _ovl(a, b):
        return a[0] < b[1] and b[0] < a[1] and a[2] < b[3] and b[2] < a[3]

    def _deps(self, engine, rboxes, wboxes):
        deps = {}

        def need(sname, c):
            if sname == engine and engine == "tensor":
                return
            if c > deps.get(sname, 0):
                deps[sname] = c

        for name, box in rboxes:
            for sn, c in self.pruned.get(name, {}).items():
                need(sn, c)
            for (sn, c, b) in self.writers.get(name, ()):
                if self._ovl(box, b):
                    need(sn, c)
        for name, box in wboxes:
            for sn, c in self.pruned.get(name, {}).items():
                need(sn, c)
            for (sn, c, b) in self.writers.get(name, ()):
                if self._ovl(box, b):
                    need(sn, c)
            for (sn, c, b) in self.readers.get(name, ()):
                if self._ovl(box, b):
                    need(sn, c)
        return deps

    def _record(self, token, rboxes, wboxes):
        sn, c = token
        for name, box in rboxes:
            lst = self.readers.setdefault(name, [])
            lst.append((sn, c, box))
            if len(lst) > self.KEEP:
                s0, c0, _ = lst.pop(0)
                pm = self.pruned.setdefault(name, {})
                if c0 > pm.get(s0, 0):
                    pm[s0] = c0
        for name, box in wboxes:
            lst = self.writers.setdefault(name, [])
            lst.append((sn, c, box))
            if len(lst) > self.KEEP:
                s0, c0, _ = lst.pop(0)
                pm = self.pruned.setdefault(name, {})
                if c0 > pm.get(s0, 0):
                    pm[s0] = c0

    def _effwaits(self, engine, deps):
        w = self.waited[engine]
        eff = []
        for sn, c in sorted(deps.items()):
            if c > w.get(sn, 0):
                eff.append((sn, c))
                w[sn] = c
        return eff

    def emit(self, engine, fn, inc=1, rd=(), wr=()):
        rboxes = [self._box(ap) for ap in rd]
        wboxes = [self._box(ap) for ap in wr]
        deps = self._deps(engine, rboxes, wboxes)
        if inc == 16:
            k = self.dnext
            self.dnext = (self.dnext + 1) % len(self.dsems)
            prev = self.duse[k] * 16
            sname = "d%d" % k
            if prev > 0 and prev > deps.get(sname, 0):
                deps[sname] = prev
            self.duse[k] += 1
            token = (sname, self.duse[k] * 16)
            eff = self._effwaits(engine, deps)
            self.ops[engine].append((eff, fn, ("d", k)))
        else:
            self.count[engine] += inc
            token = (engine, self.count[engine])
            eff = self._effwaits(engine, deps)
            self.ops[engine].append((eff, fn, ("e", engine)))
        self._record(token, rboxes, wboxes)

    def final_barrier(self):
        deps = {e: c for e, c in self.count.items() if c > 0 and e != "sync"}
        for k, u in enumerate(self.duse):
            if u > 0:
                deps["d%d" % k] = u * 16
        eff = sorted(deps.items())
        self.count["sync"] += 1
        self.ops["sync"].append((eff, lambda e: e.nop(), ("e", "sync")))

    def _sem(self, sname):
        if sname.startswith("d") and sname[1:].isdigit():
            return self.dsems[int(sname[1:])]
        return self.esems[sname]

    def finalize(self):
        nc = self.nc
        with nc.Block() as block:
            def mk(engine_name):
                def body(eng):
                    for eff, fn, tok in self.ops[engine_name]:
                        for sn, c in eff:
                            eng.wait_ge(self._sem(sn), c)
                        inst = fn(eng)
                        if tok[0] == "d":
                            inst.then_inc(self.dsems[tok[1]], 16)
                        else:
                            inst.then_inc(self.esems[tok[1]], 1)
                return body
            block.sync(mk("sync"))
            block.tensor(mk("tensor"))
            block.scalar(mk("scalar"))
            block.vector(mk("vector"))
            block.gpsimd(mk("gpsimd"))


def _aps(*xs):
    return [x for x in xs if x is not None and hasattr(x, "tensor")]


class H:

    def __init__(self, ser):
        self.E = ser.emit

    def DMA(self, out, in_, q="sync"):
        self.E(q, lambda e, o=out, i=in_: e.dma_start(out=o, in_=i), 16,
               rd=_aps(in_), wr=_aps(out))

    def DMAT(self, out, in_, q="scalar"):
        self.E(q, lambda e, o=out, i=in_: e.dma_start_transpose(out=o, in_=i), 16,
               rd=_aps(in_), wr=_aps(out))

    def MM(self, out, l, r, start=True, stop=True):
        rd = _aps(l, r) if start else _aps(l, r, out)
        self.E("tensor", lambda e, o=out, a=l, b=r, s=start, p=stop:
               e.matmul(o, a, b, start=s, stop=p), 1, rd=rd, wr=_aps(out))

    def TR(self, out, in_, ident):
        self.E("tensor", lambda e, o=out, i=in_, d=ident: e.transpose(o, i, d), 1,
               rd=_aps(in_, ident), wr=_aps(out))

    def CP(self, out, in_):
        self.E("vector", lambda e, o=out, i=in_: e.tensor_copy(out=o, in_=i), 1,
               rd=_aps(in_), wr=_aps(out))

    def ACT(self, out, in_, func, bias=None, scale=None, accum=None):
        kw = {}
        if bias is not None:
            kw["bias"] = bias
        if scale is not None:
            kw["scale"] = scale
        if accum is not None:
            kw["accum_out"] = accum
        self.E("scalar", lambda e, o=out, i=in_, f=func, k=kw:
               e.activation(out=o, in_=i, func=f, **k), 1,
               rd=_aps(in_, bias if hasattr(bias, "tensor") else None,
                       scale if hasattr(scale, "tensor") else None),
               wr=_aps(out, accum))

    def SMUL(self, out, in_, c):
        self.E("scalar", lambda e, o=out, i=in_, m=c: e.mul(o, i, m), 1,
               rd=_aps(in_), wr=_aps(out))

    def TT(self, out, in0, in1, op):
        self.E("vector", lambda e, o=out, a=in0, b=in1, p=op:
               e.tensor_tensor(out=o, in0=a, in1=b, op=p), 1,
               rd=_aps(in0, in1), wr=_aps(out))

    def TSMUL(self, out, in0, s1):
        self.E("vector", lambda e, o=out, a=in0, s=s1: e.tensor_scalar_mul(out=o, in0=a, scalar1=s), 1,
               rd=_aps(in0, s1 if hasattr(s1, "tensor") else None), wr=_aps(out))

    def TS(self, out, in0, s1, s2, op0, op1):
        self.E("vector", lambda e, o=out, a=in0, x=s1, y=s2, p0=op0, p1=op1:
               e.tensor_scalar(out=o, in0=a, scalar1=x, scalar2=y, op0=p0, op1=p1), 1,
               rd=_aps(in0, s1 if hasattr(s1, "tensor") else None,
                       s2 if hasattr(s2, "tensor") else None), wr=_aps(out))

    def TSMAX(self, out, in0, s1):
        self.E("vector", lambda e, o=out, a=in0, s=s1: e.tensor_scalar_max(out=o, in0=a, scalar1=s), 1,
               rd=_aps(in0, s1 if hasattr(s1, "tensor") else None), wr=_aps(out))

    def TSSUB(self, out, in0, s1):
        self.E("vector", lambda e, o=out, a=in0, s=s1: e.tensor_scalar_sub(out=o, in0=a, scalar1=s), 1,
               rd=_aps(in0, s1 if hasattr(s1, "tensor") else None), wr=_aps(out))

    def TSADD(self, out, in0, s1):
        self.E("vector", lambda e, o=out, a=in0, s=s1: e.tensor_scalar_add(out=o, in0=a, scalar1=s), 1,
               rd=_aps(in0, s1 if hasattr(s1, "tensor") else None), wr=_aps(out))

    def RSUM(self, out, in_):
        self.E("vector", lambda e, o=out, i=in_: e.reduce_sum(o, i, axis=AX.X), 1,
               rd=_aps(in_), wr=_aps(out))

    def RMAX(self, out, in_, absv=None):
        self.E("vector", lambda e, o=out, i=in_, a=absv:
               e.tensor_reduce(o, i, axis=AX.X, op=ALU.max, apply_absolute_value=a), 1,
               rd=_aps(in_), wr=_aps(out))

    def RADD(self, out, in_, absv=None):
        self.E("vector", lambda e, o=out, i=in_, a=absv:
               e.tensor_reduce(o, i, axis=AX.X, op=ALU.add, apply_absolute_value=a), 1,
               rd=_aps(in_), wr=_aps(out))

    def RECIP(self, out, in_):
        self.E("vector", lambda e, o=out, i=in_: e.reciprocal(o, i), 1,
               rd=_aps(in_), wr=_aps(out))

    def LN(self, t_ap, out_ap, xc, sq, stats, W=1024, eps=1e-5, out_scale=None):
        c = 1.0 if out_scale is None else out_scale
        self.RSUM(stats[:, 0:1], t_ap)
        self.SMUL(stats[:, 1:2], stats[:, 0:1], 1.0 / W)
        self.TSSUB(xc, t_ap, stats[:, 1:2])
        self.ACT(sq, xc, AF.Square, accum=stats[:, 2:3])
        self.TS(stats[:, 3:4], stats[:, 2:3], c * c / W, float(eps), ALU.mult, ALU.add)
        self.ACT(stats[:, 4:5], stats[:, 3:4], AF.Sqrt)
        self.RECIP(stats[:, 5:6], stats[:, 4:5])
        if out_scale is not None:
            self.SMUL(stats[:, 6:7], stats[:, 5:6], c)
            self.TSMUL(out_ap, xc, stats[:, 6:7])
        else:
            self.TSMUL(out_ap, xc, stats[:, 5:6])


def build_megakernel(dbg=False):
    nc = bass.Bass("TRN2", target_bir_lowering=False)
    x = nc.dram_tensor("x", [S, D], F32, kind="ExternalInput").ap()
    Wq = nc.dram_tensor("Wq", [D, D], F32, kind="ExternalInput").ap()
    Wk = nc.dram_tensor("Wk", [D, D], F32, kind="ExternalInput").ap()
    Wv = nc.dram_tensor("Wv", [D, D], F32, kind="ExternalInput").ap()
    Wo = nc.dram_tensor("Wo", [D, D], F32, kind="ExternalInput").ap()
    w1w = nc.dram_tensor("w1w", [DH, D], F32, kind="ExternalInput").ap()
    w1s = nc.dram_tensor("w1s", [DH], F32, kind="ExternalInput").ap()
    w2w = nc.dram_tensor("w2w", [DH, D], F32, kind="ExternalInput").ap()
    w2s = nc.dram_tensor("w2s", [DH], F32, kind="ExternalInput").ap()
    h1w = nc.dram_tensor("h1w", [DH, DH], F32, kind="ExternalInput").ap()
    h1s = nc.dram_tensor("h1s", [DH], F32, kind="ExternalInput").ap()
    h2w = nc.dram_tensor("h2w", [DH, DH], F32, kind="ExternalInput").ap()
    h2s = nc.dram_tensor("h2s", [DH], F32, kind="ExternalInput").ap()
    w3w = nc.dram_tensor("w3w", [D, DH], F32, kind="ExternalInput").ap()
    w3s = nc.dram_tensor("w3s", [D], F32, kind="ExternalInput").ap()
    Qh = nc.dram_tensor("Qh", [64, D, 16], F32, kind="ExternalInput").ap()
    Kh = nc.dram_tensor("Kh", [64, D, 16], F32, kind="ExternalInput").ap()
    Vh = nc.dram_tensor("Vh", [64, D, 16], F32, kind="ExternalInput").ap()
    Wh = nc.dram_tensor("Wh", [D, 64, 16], F32, kind="ExternalInput").ap()
    wgt = nc.dram_tensor("wgt", [D, D], F32, kind="ExternalInput").ap()
    lng = nc.dram_tensor("lng", [D], F32, kind="ExternalInput").ap()
    lnb = nc.dram_tensor("lnb", [D], F32, kind="ExternalInput").ap()
    identb = nc.dram_tensor("identb", [128, 128], BF16, kind="ExternalInput").ap()
    identf = nc.dram_tensor("identf", [128, 128], F32, kind="ExternalInput").ap()
    onesb = nc.dram_tensor("onesb", [128, 1], BF16, kind="ExternalInput").ap()
    onesf = nc.dram_tensor("onesf", [1, 128], F32, kind="ExternalInput").ap()
    cmask = nc.dram_tensor("cmask", [128, 896], BF16, kind="ExternalInput").ap()
    out = nc.dram_tensor("out", [S, D], F32, kind="ExternalOutput").ap()

    _k = "ExternalOutput" if dbg else "Internal"
    x1d = nc.dram_tensor("x1d", [S, D], F32, kind=_k).ap()
    x2d = nc.dram_tensor("x2d", [S, D], F32, kind=_k).ap()
    x3d = nc.dram_tensor("x3d", [S, D], F32, kind=_k).ap()
    zd = nc.dram_tensor("zd", [S, D], F32, kind=_k).ap()
    bTd = nc.dram_tensor("bTd", [NHC, 128, S], BF16, kind=_k).ap()
    cTd = nc.dram_tensor("cTd", [NHC, 128, S], BF16, kind=_k).ap()
    aTd = nc.dram_tensor("aTd", [NHC, 128, S], BF16, kind=_k).ap() if dbg else None

    SL2 = 18432
    with contextlib.ExitStack() as st:
        ab = st.enter_context(nc.sbuf_tensor("ab", [128, 65536 + SL2], BF16))
        af = st.enter_context(nc.sbuf_tensor("af", [128, 9216], F32))
        idb = st.enter_context(nc.sbuf_tensor("idb", [128, 128], BF16))
        idf = st.enter_context(nc.sbuf_tensor("idf", [128, 128], F32))
        onb = st.enter_context(nc.sbuf_tensor("onb", [128, 1], BF16))
        onf = st.enter_context(nc.sbuf_tensor("onf", [1, 128], F32))
        cmk = st.enter_context(nc.sbuf_tensor("cmk", [128, 896], BF16))
        stats = st.enter_context(nc.sbuf_tensor("stats", [128, 8], F32))
        xnb = st.enter_context(nc.sbuf_tensor("xnb", [128, D], BF16))
        p0 = st.enter_context(nc.sbuf_tensor("p0", [128, 512], BF16))
        s1a = st.enter_context(nc.sbuf_tensor("s1a", [128, 32], F32))
        s2a = st.enter_context(nc.sbuf_tensor("s2a", [128, 32], F32))
        mcols = st.enter_context(nc.sbuf_tensor("mcols", [128, 16], F32))
        mpart = st.enter_context(nc.sbuf_tensor("mpart", [128, 1], F32))
        m1 = st.enter_context(nc.sbuf_tensor("m1", [1, 1], F32))
        sc = st.enter_context(nc.sbuf_tensor("sc", [1, 16], F32))
        apart = st.enter_context(nc.sbuf_tensor("apart", [128, 1], F32))
        rbp = st.enter_context(nc.sbuf_tensor("rbp", [128, 1], F32))
        c1p = st.enter_context(nc.sbuf_tensor("c1p", [128, 1], F32))
        fsp = st.enter_context(nc.sbuf_tensor("fsp", [128, 1], F32))
        ps_trb = st.enter_context(nc.psum_tensor("ps_trb", [128, 128], BF16))
        ps_trb2 = st.enter_context(nc.psum_tensor("ps_trb2", [128, 128], BF16))
        p1 = st.enter_context(nc.psum_tensor("p1", [128, 512], F32))
        p2 = st.enter_context(nc.psum_tensor("p2", [128, 512], F32))
        p3 = st.enter_context(nc.psum_tensor("p3", [128, 512], F32))
        p4 = st.enter_context(nc.psum_tensor("p4", [1, 512], F32))
        p5 = st.enter_context(nc.psum_tensor("p5", [64, 512], F32))
        p7 = st.enter_context(nc.psum_tensor("p7", [128, 512], F32))
        esems = {e: st.enter_context(nc.semaphore("sem_" + e)) for e in ENGINES}
        dsems = [st.enter_context(nc.semaphore("dma_%d" % i)) for i in range(DepSer.NDMA)]

        ser = DepSer(nc, esems, dsems)
        h = H(ser)

        def SB(a, b):
            return ab[:, a:b]
        S0, S1B, S2B = 0, 32768, 65536

        def FS(i, n=1):
            return af[:, i * 1024:(i + n) * 1024]

        pTa = p0[:, 0:256]
        pTb = p0[:, 256:512]
        cfv = af[0:64, 4096:4608]
        cf2 = cfv[:, 0:256]
        rden = af[0:1, 4608:5120]
        rden2 = rden[:, 0:256]
        gbv = af[0:1, 5120:6144]
        qsflat = af[0:64, 5120:7168]
        qview = af[0:64, 5120:7168].rearrange("p (d h) -> p d h", h=16)
        wsflat = af[:, 5120:6144]
        wview = af[:, 5120:6144].rearrange("p (v h) -> p v h", h=16)
        xqb = ab[:, 65536 + 8192: 65536 + 9216]
        xqT = ab[:, 65536 + 9216: 65536 + 10240]
        exb = ab[0:64, 81920:82048]
        wrb = ab[:, 82048:82176]
        svs = af[0:32, 5120:5248]
        mrow = af[0:1, 6144:6272]
        arow = af[0:1, 3072:3200]
        acol = af[:, 3200:3208]

        h.DMA(idb[:], identb[:, :])
        h.DMA(idf[:], identf[:, :])
        h.DMA(onb[:], onesb[:, :])
        h.DMA(onf[:], onesf[:, :])
        h.DMA(cmk[:], cmask[:, :])

        xnT = SB(S0, S0 + 16384)
        qT = SB(S0 + 16384, S0 + 32768)
        kT = SB(S1B, S1B + 16384)
        vN = SB(S1B + 16384, S1B + 32768)
        wbf = SB(S2B, S2B + 8192)
        ctxT = SB(S2B + 8192, S2B + 10240)
        xt, xc, sqs, comb = FS(0), FS(1), FS(2), FS(3)
        wstg1 = FS(4)
        wob = FS(5, 4)

        def ln_tileT(src_dram, ti, dstT):
            h.DMA(xt[:], src_dram[ti * 128:(ti + 1) * 128, :])
            h.LN(xt[:], xnb[:], xc[:], sqs[:], stats)
            for dc in range(NDC):
                tb = (ps_trb if dc % 2 == 0 else ps_trb2)[:]
                h.TR(tb, xnb[:, dc * 128:(dc + 1) * 128], idb[:])
                h.CP(dstT[:, dc * S + ti * 128: dc * S + (ti + 1) * 128], tb)

        for ti in range(NT):
            ln_tileT(x, ti, xnT)

        def load_w_nat(w_ap, dst):
            for t in range(NDC):
                h.DMA(wstg1[:], w_ap[t * 128:(t + 1) * 128, :])
                h.CP(dst[:, t * D:(t + 1) * D], wstg1[:])

        def proj_T(dst):
            for et in range(NDC):
                for sch in range(4):
                    P = p1 if (et * 4 + sch) % 2 == 0 else p7
                    for dc in range(NDC):
                        h.MM(P[:], wbf[:, dc * D + et * 128: dc * D + (et + 1) * 128],
                             xnT[:, dc * S + sch * 512: dc * S + (sch + 1) * 512],
                             start=(dc == 0), stop=(dc == NDC - 1))
                    h.ACT(dst[:, et * S + sch * 512: et * S + (sch + 1) * 512], P[:], AF.Copy)

        load_w_nat(Wq, wbf)
        proj_T(qT)
        load_w_nat(Wk, wbf)
        proj_T(kT)
        load_w_nat(Wv, wbf)
        for ti in range(NT):
            for ec in range(2):
                P = p1 if (ti * 2 + ec) % 2 == 0 else p7
                for dc in range(NDC):
                    h.MM(P[:], xnT[:, dc * S + ti * 128: dc * S + (ti + 1) * 128],
                         wbf[:, dc * D + ec * 512: dc * D + (ec + 1) * 512],
                         start=(dc == 0), stop=(dc == NDC - 1))
                h.ACT(vN[:, ti * D + ec * 512: ti * D + (ec + 1) * 512], P[:], AF.Copy)
        load_w_nat(Wo, wbf)

        def emit_chunk_A(cidx, half_ap, other_ap, scale_c):
            if other_ap is None:
                h.LN(half_ap, comb[:], xc[:], sqs[:], stats, out_scale=scale_c)
            else:
                h.TT(comb[:], half_ap, other_ap, ALU.add)
                h.LN(comb[:], comb[:], xc[:], sqs[:], stats, out_scale=scale_c)
            h.DMA(xt[:], x[cidx * 128:(cidx + 1) * 128, :])
            h.TT(xc[:], comb[:], xt[:], ALU.add)
            h.DMA(x1d[cidx * 128:(cidx + 1) * 128, :], xc[:], q="scalar")

        for n in range(NW):
            base = (n % 2) * 2 * D
            pbase = ((n + 1) % 2) * 2 * D
            s0q = n * 128
            for hh in range(16):
                hc, hb = hh // 2, (hh % 2) * 64
                for kt_i, pbuf, PS in ((0, pTa, p2), (1, pTb, p7)):
                    kk = s0q + kt_i * 128
                    h.MM(PS[:, 0:256], kT[hb:hb + 64, hc * S + kk: hc * S + kk + 128],
                         qT[hb:hb + 64, hc * S + s0q: hc * S + s0q + 256])
                    h.ACT(pbuf, PS[:, 0:256], AF.Exp, scale=0.125)
                h.MM(p4[:, 0:256], onb[:, :], pTa, start=True, stop=False)
                h.MM(p4[:, 0:256], onb[:, :], pTb, start=False, stop=True)
                h.RECIP(rden2, p4[:, 0:256])
                h.MM(p5[:, 0:256], onf[0:1, 0:64], rden2)
                for kt_i, pbuf in ((0, pTa), (1, pTb)):
                    h.MM(p3[hb:hb + 64, 0:256],
                         vN[:, (n + kt_i) * D + hh * 64: (n + kt_i) * D + hh * 64 + 64],
                         pbuf, start=(kt_i == 0), stop=(kt_i == 1))
                h.CP(cf2, p3[hb:hb + 64, 0:256])
                h.TT(ctxT[hb:hb + 64, hc * 256:(hc + 1) * 256], cf2, p5[:, 0:256], ALU.mult)
            for qh in range(2):
                for ec in range(2):
                    for dc in range(NDC):
                        h.MM(p1[:], ctxT[:, dc * 256 + qh * 128: dc * 256 + (qh + 1) * 128],
                             wbf[:, dc * D + ec * 512: dc * D + (ec + 1) * 512],
                             start=(dc == 0), stop=(dc == NDC - 1))
                    h.CP(wob[:, base + qh * D + ec * 512: base + qh * D + (ec + 1) * 512], p1[:])
            if n == 0:
                emit_chunk_A(0, wob[:, base: base + D], None, 1.0)
            else:
                emit_chunk_A(n, wob[:, base: base + D], wob[:, pbase + D: pbase + 2 * D], 0.5)
        lastb = ((NW - 1) % 2) * 2 * D
        emit_chunk_A(NW, wob[:, lastb + D: lastb + 2 * D], None, 1.0)

        aT = SB(S0, S0 + 65536)
        xnTB = SB(S2B + 2048, S2B + SL2)
        w1T = SB(S2B, S2B + 1024)
        w2T = SB(S2B + 1024, S2B + 2048)
        tA, tB2 = FS(3)[:, 0:512], FS(3)[:, 512:1024]

        for sv, dst in ((w1s, s1a), (w2s, s2a)):
            h.DMA(svs, sv.rearrange("(t f) -> t f", t=32))
            h.TR(p2[:, 0:32], svs, idf[0:32, 0:32])
            h.CP(dst[:], p2[:, 0:32])
        for ti in range(NT):
            ln_tileT(x1d, ti, xnTB)
        for t in range(NHC):
            h.DMA(wstg1[:], w1w[t * 128:(t + 1) * 128, :])
            h.CP(xnb[:], wstg1[:])
            for dc in range(NDC):
                tb = (ps_trb if dc % 2 == 0 else ps_trb2)[:]
                h.TR(tb, xnb[:, dc * 128:(dc + 1) * 128], idb[:])
                h.CP(w1T[:, dc * 128:(dc + 1) * 128], tb)
            h.DMA(wstg1[:], w2w[t * 128:(t + 1) * 128, :])
            h.CP(xnb[:], wstg1[:])
            for dc in range(NDC):
                tb = (ps_trb if dc % 2 == 0 else ps_trb2)[:]
                h.TR(tb, xnb[:, dc * 128:(dc + 1) * 128], idb[:])
                h.CP(w2T[:, dc * 128:(dc + 1) * 128], tb)
            for sch in range(4):
                P1 = p1 if sch % 2 == 0 else p7
                P2 = p2 if sch % 2 == 0 else p3
                for dc in range(NDC):
                    h.MM(P1[:], w1T[:, dc * 128:(dc + 1) * 128],
                         xnTB[:, dc * S + sch * 512: dc * S + (sch + 1) * 512],
                         start=(dc == 0), stop=(dc == NDC - 1))
                for dc in range(NDC):
                    h.MM(P2[:], w2T[:, dc * 128:(dc + 1) * 128],
                         xnTB[:, dc * S + sch * 512: dc * S + (sch + 1) * 512],
                         start=(dc == 0), stop=(dc == NDC - 1))
                h.ACT(tA[:], P1[:], AF.Silu, scale=s1a[:, t:t + 1])
                h.TSMUL(tB2[:], P2[:], s2a[:, t:t + 1])
                h.TT(aT[:, t * S + sch * 512: t * S + (sch + 1) * 512], tA[:], tB2[:], ALU.mult)

        hinS = SB(S0, S0 + 65536)
        wTh = SB(S2B, S2B + 4096)
        wbfh = SB(S2B + 4096, S2B + 8192)
        wstg4 = FS(5, 4)
        obf = SB(S2B + 8192, S2B + 8704)
        obf2 = SB(S2B + 8704, S2B + 9216)

        def hidden_layer(houtd, hwt, hst):
            h.DMA(svs, hst.rearrange("(t f) -> t f", t=32))
            h.TR(p2[:, 0:32], svs, idf[0:32, 0:32])
            h.CP(s1a[:], p2[:, 0:32])
            for t in range(NHC):
                for q4 in range(4):
                    h.DMA(wstg4[:, q4 * D:(q4 + 1) * D], hwt[t * 128:(t + 1) * 128, q4 * D:(q4 + 1) * D])
                h.CP(wbfh[:], wstg4[:])
                for ic in range(NHC):
                    tb = (ps_trb if ic % 2 == 0 else ps_trb2)[:]
                    h.TR(tb, wbfh[:, ic * 128:(ic + 1) * 128], idb[:])
                    h.CP(wTh[:, ic * 128:(ic + 1) * 128], tb)
                for sch in range(4):
                    P = p1 if sch % 2 == 0 else p7
                    ob = obf if sch % 2 == 0 else obf2
                    for ic in range(NHC):
                        h.MM(P[:], wTh[:, ic * 128:(ic + 1) * 128],
                             hinS[:, ic * S + sch * 512: ic * S + (sch + 1) * 512],
                             start=(ic == 0), stop=(ic == NHC - 1))
                    h.ACT(ob, P[:], AF.Gelu, scale=s1a[:, t:t + 1])
                    h.DMA(houtd[t, :, sch * 512:(sch + 1) * 512], ob, q="scalar")

        if dbg:
            for ic in range(NHC):
                h.DMA(aTd[ic, :, :], aT[:, ic * S:(ic + 1) * S], q="scalar")
        hidden_layer(bTd, h1w, h1s)
        for ic in range(NHC):
            h.DMA(hinS[:, ic * S:(ic + 1) * S], bTd[ic, :, :])
        hidden_layer(cTd, h2w, h2s)

        w3R = SB(S1B, S1B + 32768)
        cS = SB(S2B + 8192, S2B + 12288)
        cS2 = SB(S2B + 12288, S2B + 16384)
        ys = FS(3)
        s3B = FS(4)

        h.DMA(gbv, w3s[None, :])
        for ec in range(2):
            h.MM(p1[:], onf[0:1, :], gbv[:, ec * 512:(ec + 1) * 512])
            h.CP(s3B[:, ec * 512:(ec + 1) * 512], p1[:])
        for et in range(NDC):
            for q4 in range(4):
                h.DMA(wstg4[:, q4 * D:(q4 + 1) * D], w3w[et * 128:(et + 1) * 128, q4 * D:(q4 + 1) * D])
            h.CP(wbfh[:], wstg4[:])
            for ic in range(NHC):
                tb = (ps_trb if ic % 2 == 0 else ps_trb2)[:]
                h.TR(tb, wbfh[:, ic * 128:(ic + 1) * 128], idb[:])
                h.CP(w3R[:, ic * D + et * 128: ic * D + (et + 1) * 128], tb)
        for ti in range(NT):
            cSt = cS if ti % 2 == 0 else cS2
            for ic in range(NHC):
                h.DMA(cSt[:, ic * 128:(ic + 1) * 128], cTd[ic, :, ti * 128:(ti + 1) * 128])
            for ec in range(2):
                P = p1 if ec == 0 else p7
                for ic in range(NHC):
                    h.MM(P[:], cSt[:, ic * 128:(ic + 1) * 128],
                         w3R[:, ic * D + ec * 512: ic * D + (ec + 1) * 512],
                         start=(ic == 0), stop=(ic == NHC - 1))
                h.TT(ys[:, ec * 512:(ec + 1) * 512], s3B[:, ec * 512:(ec + 1) * 512], P[:], ALU.mult)
            h.LN(ys[:], comb[:], xc[:], sqs[:], stats)
            h.DMA(xt[:], x1d[ti * 128:(ti + 1) * 128, :])
            h.TT(xc[:], comb[:], xt[:], ALU.add)
            h.DMA(x2d[ti * 128:(ti + 1) * 128, :], xc[:], q="scalar")

        bufA = SB(S0, S0 + 16384)
        qTs = SB(S0 + 16384, S0 + 32768)
        kTs = SB(S1B, S1B + 16384)
        vS = SB(S1B + 16384, S1B + 32768)
        bufE = SB(S2B, S2B + 8192)
        bufF = SB(S2B + 8192, S2B + 16384)
        ztF = FS(3)
        SQ = 0.25

        for ti in range(NT):
            ln_tileT(x2d, ti, bufA)

        def prep_qk(src, dstT):
            for dc in range(8):
                h.DMA(qsflat, src[:, dc * 128:(dc + 1) * 128, :].rearrange("q d h -> q (d h)"))
                for hh in range(16):
                    tb = (ps_trb if hh % 2 == 0 else ps_trb2)[:, 0:64]
                    h.CP(exb, qview[:, :, hh])
                    h.TR(tb, exb, idb[0:64, 0:64])
                    h.CP(bufF[:, (hh * 8 + dc) * 64:(hh * 8 + dc + 1) * 64], tb)
            for hh in range(16):
                hc, hb = hh // 2, (hh % 2) * 64
                for j in range(4):
                    P = p1 if j % 2 == 0 else p7
                    for dc in range(NDC):
                        h.MM(P[0:64, :], bufF[:, (hh * 8 + dc) * 64:(hh * 8 + dc + 1) * 64],
                             bufA[:, dc * S + j * 512: dc * S + (j + 1) * 512],
                             start=(dc == 0), stop=(dc == NDC - 1))
                    h.ACT(dstT[hb:hb + 64, hc * S + j * 512: hc * S + (j + 1) * 512],
                          P[0:64, :], AF.Copy, scale=SQ)

        prep_qk(Qh, qTs)
        prep_qk(Kh, kTs)

        for dc in range(8):
            h.DMA(qsflat, Vh[:, dc * 128:(dc + 1) * 128, :].rearrange("q d h -> q (d h)"))
            for hh in range(16):
                tb = (ps_trb if hh % 2 == 0 else ps_trb2)[:, 0:64]
                h.CP(exb, qview[:, :, hh])
                h.TR(tb, exb, idb[0:64, 0:64])
                h.CP(bufE[:, dc * D + hh * 64: dc * D + (hh + 1) * 64], tb)
        for ti in range(NT):
            for ec in range(2):
                P = p1 if (ti * 2 + ec) % 2 == 0 else p7
                for dc in range(NDC):
                    h.MM(P[:], bufA[:, dc * S + ti * 128: dc * S + (ti + 1) * 128],
                         bufE[:, dc * D + ec * 512: dc * D + (ec + 1) * 512],
                         start=(dc == 0), stop=(dc == NDC - 1))
                h.ACT(vS[:, ti * D + ec * 512: ti * D + (ec + 1) * 512], P[:], AF.Copy, scale=SQ)
        for dc in range(NDC):
            h.DMA(wsflat, Wh[dc * 128:(dc + 1) * 128, :, :].rearrange("d v h -> d (v h)"))
            for b in range(8):
                tb = (ps_trb if b % 2 == 0 else ps_trb2)[:]
                h.CP(wrb[:, 0:64], wview[:, :, 2 * b])
                h.CP(wrb[:, 64:128], wview[:, :, 2 * b + 1])
                h.TR(tb, wrb, idb[:])
                h.CP(bufE[:, b * D + dc * 128: b * D + (dc + 1) * 128], tb)

        for hh in range(16):
            hc, hb = hh // 2, (hh % 2) * 64
            for j in range(4):
                nC = 4 * (j + 1)
                for c in range(nC):
                    PS = p2 if c % 2 == 0 else p7
                    h.MM(PS[:], kTs[hb:hb + 64, hc * S + c * 128: hc * S + (c + 1) * 128],
                         qTs[hb:hb + 64, hc * S + j * 512: hc * S + (j + 1) * 512])
                    o = c * 128 - j * 512
                    if o >= 0:
                        h.ACT(p0[:], PS[:], AF.Exp, scale=1.0 / 64.0)
                        h.TT(bufF[:, c * 512:(c + 1) * 512], p0[:], cmk[:, 384 - o: 896 - o], ALU.mult)
                    else:
                        h.ACT(bufF[:, c * 512:(c + 1) * 512], PS[:], AF.Exp, scale=1.0 / 64.0)
                for c in range(nC):
                    h.MM(p4[:], onb[:, :], bufF[:, c * 512:(c + 1) * 512],
                         start=(c == 0), stop=(c == nC - 1))
                h.RECIP(rden, p4[:])
                h.MM(p5[:], onf[0:1, 0:64], rden)
                cb = (hh % 2) * 64
                for c in range(nC):
                    h.MM(p3[cb:cb + 64, :], vS[:, c * D + hh * 64: c * D + (hh + 1) * 64],
                         bufF[:, c * 512:(c + 1) * 512], start=(c == 0), stop=(c == nC - 1))
                h.CP(cfv, p3[cb:cb + 64, :])
                h.TT(bufA[hb:hb + 64, hc * S + j * 512: hc * S + (j + 1) * 512], cfv, p5[:], ALU.mult)

        for ti in range(NT):
            for ec in range(2):
                P = p1 if ec == 0 else p7
                for dc in range(NDC):
                    h.MM(P[:], bufA[:, dc * S + ti * 128: dc * S + (ti + 1) * 128],
                         bufE[:, dc * D + ec * 512: dc * D + (ec + 1) * 512],
                         start=(dc == 0), stop=(dc == NDC - 1))
                h.ACT(ztF[:, ec * 512:(ec + 1) * 512], P[:], AF.Copy, scale=0.25)
            h.LN(ztF[:], comb[:], xc[:], sqs[:], stats)
            h.DMA(xt[:], x2d[ti * 128:(ti + 1) * 128, :])
            h.TT(xc[:], comb[:], xt[:], ALU.add)
            h.DMA(x3d[ti * 128:(ti + 1) * 128, :], xc[:], q="scalar")

        xn1, xn2, ztG, zt2 = FS(3), FS(4), FS(5), FS(6)
        Gt, Bt = FS(7), FS(8)
        h.DMA(gbv, lng[None, :])
        for ec in range(2):
            h.MM(p1[:], onf[0:1, :], gbv[:, ec * 512:(ec + 1) * 512])
            h.CP(Gt[:, ec * 512:(ec + 1) * 512], p1[:])
        h.DMA(gbv, lnb[None, :])
        for ec in range(2):
            h.MM(p1[:], onf[0:1, :], gbv[:, ec * 512:(ec + 1) * 512])
            h.CP(Bt[:, ec * 512:(ec + 1) * 512], p1[:])
        for ti in range(NT):
            h.DMA(xt[:], x3d[ti * 128:(ti + 1) * 128, :])
            h.LN(xt[:], xn1[:], xc[:], sqs[:], stats)
            h.LN(xn1[:], xn2[:], xc[:], sqs[:], stats, eps=1e-3)
            h.TT(ztG[:], xn2[:], Gt[:], ALU.mult)
            h.TT(zt2[:], ztG[:], Bt[:], ALU.add)
            h.DMA(zd[ti * 128:(ti + 1) * 128, :], zt2[:], q="scalar")
            h.RMAX(mcols[:, ti:ti + 1], zt2[:], absv=True)
        h.RMAX(mpart[:], mcols[:])
        h.TR(p2[0:1, 0:128], mpart[:], idf[:])
        h.CP(mrow, p2[0:1, 0:128])
        h.RMAX(m1[:], mrow)
        h.CP(sc[:, 0:1], m1[:])

        tqf, sgn, u2, ztH, xqf, ytH = FS(3), FS(4), FS(5), FS(6), FS(7), FS(8)
        wqb = SB(S2B, S2B + 8192)
        QB = 128.0
        HIc = float(np.float32(QB) - np.float32(1e-5))

        for t in range(NDC):
            h.DMA(wstg1[:], wgt[t * 128:(t + 1) * 128, :])
            h.RADD(acol[:, t:t + 1], wstg1[:], absv=True)
        h.RSUM(apart[:], acol)
        h.TR(p2[0:1, 0:128], apart[:], idf[:])
        h.CP(arow, p2[0:1, 0:128])
        h.RSUM(sc[:, 1:2], arow)
        h.SMUL(sc[:, 2:3], sc[:, 1:2], 1.0 / (1024.0 * 1024.0))
        h.TSADD(sc[:, 3:4], sc[:, 2:3], 1e-5)
        h.RECIP(sc[:, 4:5], sc[:, 3:4])
        h.RECIP(sc[:, 5:6], sc[:, 0:1])
        h.SMUL(sc[:, 6:7], sc[:, 5:6], QB)
        h.TT(sc[:, 7:8], sc[:, 0:1], sc[:, 2:3], ALU.mult)
        h.SMUL(sc[:, 7:8], sc[:, 7:8], 1.0 / QB)
        for src_col, dst in ((4, rbp), (6, c1p), (7, fsp)):
            h.MM(p2[:, 0:1], onf[0:1, :], sc[:, src_col:src_col + 1])
            h.CP(dst[:], p2[:, 0:1])

        for t in range(NDC):
            h.DMA(wstg1[:], wgt[t * 128:(t + 1) * 128, :])
            h.TSMUL(tqf[:], wstg1[:], rbp[:])
            h.ACT(sgn[:], tqf[:], AF.Sign)
            h.ACT(u2[:], tqf[:], AF.Abs)
            h.TSSUB(tqf[:], u2[:], 0.5)
            h.ACT(u2[:], tqf[:], AF.Sign)
            h.ACT(tqf[:], u2[:], AF.Relu)
            h.TT(wqb[:, t * D:(t + 1) * D], sgn[:], tqf[:], ALU.mult)

        for ti in range(NT):
            h.DMA(ztH[:], zd[ti * 128:(ti + 1) * 128, :])
            h.TS(xqf[:], ztH[:], c1p[:], HIc, ALU.mult, ALU.min)
            h.TSMAX(xqb, xqf[:], -HIc)
            for dc in range(NDC):
                tb = (ps_trb if dc % 2 == 0 else ps_trb2)[:]
                h.TR(tb, xqb[:, dc * 128:(dc + 1) * 128], idb[:])
                h.CP(xqT[:, dc * 128:(dc + 1) * 128], tb)
            for ec in range(2):
                P = p1 if ec == 0 else p7
                for dc in range(NDC):
                    h.MM(P[:], xqT[:, dc * 128:(dc + 1) * 128],
                         wqb[:, dc * D + ec * 512: dc * D + (ec + 1) * 512],
                         start=(dc == 0), stop=(dc == NDC - 1))
                h.TSMUL(ytH[:, ec * 512:(ec + 1) * 512], P[:], fsp[:])
            h.LN(ytH[:], comb[:], xc[:], sqs[:], stats)
            h.DMA(xt[:], x3d[ti * 128:(ti + 1) * 128, :])
            h.TT(xc[:], comb[:], xt[:], ALU.add)
            h.DMA(out[ti * 128:(ti + 1) * 128, :], xc[:], q="scalar")
        ser.final_barrier()
        ser.finalize()
    return nc


def _consts():
    return {
        "identb": np.eye(128, dtype=BF),
        "identf": np.eye(128, dtype=np.float32),
        "onesb": np.ones((128, 1), dtype=BF),
        "onesf": np.ones((1, 128), dtype=np.float32),
    }


def _cmask():
    p = np.arange(128)[:, None]
    y = np.arange(896)[None, :]
    return (y >= p + 384).astype(BF)


def kernel(L0_attn_hidden_states, L0_attn_Wq, L0_attn_Wk, L0_attn_Wv, L0_attn_Wo,
           L0_ffn_w1_w, L0_ffn_w1_s, L0_ffn_w2_w, L0_ffn_w2_s,
           L0_ffn_h1_w, L0_ffn_h1_s, L0_ffn_h2_w, L0_ffn_h2_s,
           L0_ffn_w3_w, L0_ffn_w3_s,
           L1_attn_Q, L1_attn_K, L1_attn_V, L1_attn_W,
           L1_ffn_weight, L1_ffn_ln_gamma, L1_ffn_ln_beta):
    NCORES = 8
    cores = list(range(NCORES))
    cst = _consts()
    x_all = np.ascontiguousarray(np.asarray(L0_attn_hidden_states, dtype=np.float32))

    shared = {
        "Wq": np.ascontiguousarray(L0_attn_Wq), "Wk": np.ascontiguousarray(L0_attn_Wk),
        "Wv": np.ascontiguousarray(L0_attn_Wv), "Wo": np.ascontiguousarray(L0_attn_Wo),
        "w1w": np.ascontiguousarray(L0_ffn_w1_w), "w1s": np.ascontiguousarray(L0_ffn_w1_s),
        "w2w": np.ascontiguousarray(L0_ffn_w2_w), "w2s": np.ascontiguousarray(L0_ffn_w2_s),
        "h1w": np.ascontiguousarray(L0_ffn_h1_w), "h1s": np.ascontiguousarray(L0_ffn_h1_s),
        "h2w": np.ascontiguousarray(L0_ffn_h2_w), "h2s": np.ascontiguousarray(L0_ffn_h2_s),
        "w3w": np.ascontiguousarray(L0_ffn_w3_w), "w3s": np.ascontiguousarray(L0_ffn_w3_s),
        "Qh": np.ascontiguousarray(L1_attn_Q), "Kh": np.ascontiguousarray(L1_attn_K),
        "Vh": np.ascontiguousarray(L1_attn_V), "Wh": np.ascontiguousarray(L1_attn_W),
        "wgt": np.ascontiguousarray(L1_ffn_weight),
        "lng": np.ascontiguousarray(L1_ffn_ln_gamma), "lnb": np.ascontiguousarray(L1_ffn_ln_beta),
        "identb": cst["identb"], "identf": cst["identf"],
        "onesb": cst["onesb"], "onesf": cst["onesf"], "cmask": _cmask(),
    }
    nc = build_megakernel()
    im = [dict(shared, x=np.ascontiguousarray(x_all[c])) for c in cores]
    res = run_bass_kernel_spmd(nc, im, core_ids=cores)
    return np.stack([res.results[c]["out"] for c in cores], axis=0).astype(np.float32)
